# version 25
# baseline (speedup 1.0000x reference)
import numpy as np
import concourse.bass as bass
import concourse.mybir as mybir
from concourse.bass_utils import run_bass_kernel_spmd
from contextlib import ExitStack

F32 = mybir.dt.float32
BF16 = mybir.dt.bfloat16
AF = mybir.ActivationFunctionType
ALU = mybir.AluOpType
AX = mybir.AxisListType

D = 1024
DC = 8
NT = 16
KBN = 8
SEQ = 8192
NEG_BIG = -9.0e15
EPS = 1e-5
NSLOT = 8
NXB = 3

_BUILD_STAGE = "full"


class Sem:
    def __init__(self, es, nc, name):
        self.h = es.enter_context(nc.semaphore(name))
        self.n = 0


class Plan:
    ENGINES = ("sync", "gpsimd", "vector", "scalar", "tensor")

    def __init__(self):
        self.streams = {e: [] for e in self.ENGINES}
        self._lastwait = {}

    def wait(self, eng, sem, val):
        if val <= 0:
            return
        key = (eng, id(sem))
        if self._lastwait.get(key, -1) >= val:
            return
        self._lastwait[key] = val
        h = sem.h
        self.streams[eng].append(lambda e, h=h, v=val: e.wait_ge(h, v))

    def emit(self, eng, fn, inc=None, inc_n=1):
        if eng in ("vector", "scalar"):
            self.streams[eng].append(lambda e: e.drain())
        if inc is None:
            self.streams[eng].append(fn)
            return None
        inc.n += inc_n
        h = inc.h
        self.streams[eng].append(lambda e, fn=fn, h=h, n=inc_n: fn(e).then_inc(h, n))
        return inc.n


def build_program(stage):
    nc = bass.Bass("TRN2", target_bir_lowering=False)

    def dram(name, shape, dty=F32, kind="ExternalInput"):
        return nc.dram_tensor(name, shape, dty, kind=kind).ap()

    xkv = dram("xkv", [SEQ, D])
    xq = dram("xq", [NT * 128, D])
    wqh = dram("wqh", [128, DC * 1024])
    wkvh = dram("wkvh", [128, DC * 2048])
    bqh = dram("bqh", [128, 8])
    bkh = dram("bkh", [128, 8])
    bvh = dram("bvh", [1, D])
    maskh = dram("maskh", [128, 512], BF16)
    identh = dram("identh", [128, 128], BF16)
    rwgh = dram("rwgh", [128, DC * 9])
    w1p = dram("w1p", [8, 128, 8192])
    w2p = dram("w2p", [8, 128, 8192])
    wguh = dram("wguh", [128, 8 * 2048])
    dwh = dram("dwh", [128, 8192])
    out = dram("out", [NT * 128, D], kind="ExternalOutput")

    es = ExitStack()
    sb0 = lambda name, shape, dty=F32: es.enter_context(nc.sbuf_tensor(name, shape, dty))
    ps0 = lambda name, shape: es.enter_context(nc.psum_tensor(name, shape, F32))

    O = [sb0(f"O{t}", [128, 1024]) for t in range(NT)]
    sqdump = sb0("sqdump", [128, 1024], BF16)
    epsb = sb0("epsb", [128, 1])
    sx = sb0("sx", [128, NSLOT]); mu = sb0("mu", [128, NSLOT]); sxx = sb0("sxx", [128, NSLOT])
    var = sb0("var", [128, NSLOT]); isg = sb0("isg", [128, NSLOT]); nb2 = sb0("nb2", [128, NSLOT])
    nm = sb0("nm", [128, NT]); lden = sb0("lden", [128, NT])
    nmnew = sb0("nmnew", [128, 2]); corr = sb0("corr", [128, 2])
    lb = sb0("lb", [128, 2])
    invl = sb0("invl", [128, 2])
    bq32 = sb0("bq32", [128, 8]); bksb = sb0("bksb", [128, 8])
    c_sb = sb0("c_sb", [128, NT * 8]); g_sb = sb0("g_sb", [128, NT])
    rsc = sb0("rsc", [128, 96])
    rwg_sb = sb0("rwg_sb", [128, DC * 9], BF16)
    ones1 = sb0("ones1", [1, 128], BF16)
    bvbf = sb0("bvbf", [1, D], BF16)

    sp = [ps0(f"sp{i}", [128, 1024]) for i in range(2)]
    pO = [ps0(f"pO{i}", [128, 1024]) for i in range(2)]

    ses = ExitStack()
    mk = lambda name: Sem(ses, nc, name)
    LDC = mk("LDC"); LDM = mk("LDM"); ST = mk("ST")
    LDX = [mk(f"LDX{i}") for i in range(NXB)]
    WQ = mk("WQ"); WKV = mk("WKV"); WV = mk("WV"); RG = mk("RG"); DW = mk("DW")
    GU = [mk("GU0"), mk("GU1")]; GUB = [mk("GU0b"), mk("GU1b")]
    W1S = [mk("W10"), mk("W11")]; W2S = mk("W2S")
    TRDs = [mk(f"TRD{i}") for i in range(3)]
    PTR = [mk("PTR0"), mk("PTR1")]
    V = mk("V"); SC = mk("SC"); T = mk("T"); GP = mk("GP")

    p = Plan()

    p.emit("sync", lambda e: e.dma_start(out=bq32[:], in_=bqh), inc=LDC, inc_n=16)
    p.emit("sync", lambda e: e.dma_start(out=bksb[:], in_=bkh), inc=LDC, inc_n=16)
    ld_const = LDC.n
    p.emit("gpsimd", lambda e: e.dma_start(out=rwg_sb[:], in_=rwgh), inc=RG, inc_n=16)
    lw_rw = RG.n
    def emit_memsets():
        p.emit("gpsimd", lambda e: e.memset(epsb[:], EPS), inc=GP)
        p.emit("gpsimd", lambda e: e.memset(nm[:], 1e30), inc=GP)
        p.emit("gpsimd", lambda e: e.memset(lden[:], 0.0), inc=GP)
        p.emit("gpsimd", lambda e: e.memset(ones1[:], 1.0), inc=GP)
    gp_init = 4
    p.wait("vector", LDC, ld_const)
    p.wait("vector", GP, gp_init)
    p.wait("scalar", GP, gp_init)
    p.emit("vector", lambda e: e.tensor_scalar(out=bq32[:], in0=bq32[:], scalar1=1.0 / 32.0,
                                               scalar2=None, op0=ALU.mult), inc=V)
    v_bq32 = V.n

    skeep = ExitStack()
    Q_sb = skeep.enter_context(nc.sbuf_tensor("Q_sb", [128, 8 * 2048], BF16))
    sattn = ExitStack()
    xb = [sattn.enter_context(nc.sbuf_tensor(f"xb{i}", [128, 1024], F32)) for i in range(NXB)]
    sba = lambda name, shape, dty=F32: sattn.enter_context(nc.sbuf_tensor(name, shape, dty))
    xlnb = [sba(f"xlnb{i}", [128, 1024], BF16) for i in range(2)]
    K_sb = sba("K_sb", [128, 8 * 1024], BF16)
    V_sb = sba("V_sb", [128, 8 * 1024], BF16)
    xlnT = sba("xlnT", [128, DC * 1024], BF16)
    masksb = sba("masksb", [128, 512], BF16)
    P_sb = sba("P0", [128, 1024], BF16)
    PTd = [sba(f"PTd{r}", [128, 1024], BF16) for r in range(2)]

    sq = ExitStack()
    wq_sb = sq.enter_context(nc.sbuf_tensor("wq_sb", [128, DC * 1024], BF16))
    xlnTq = sq.enter_context(nc.sbuf_tensor("xlnTq", [128, DC * 1024], BF16))
    p.emit("gpsimd", lambda e: e.dma_start(out=wq_sb[:], in_=wqh, max_dma_last_dim=8192),
           inc=WQ, inc_n=16)
    lw_wq = WQ.n
    emit_memsets()

    slot_mu_sc = [0] * NSLOT
    slot_norm_v = [0] * NSLOT
    xslot_tr = [0, 0, 0]
    ln_xs = {}
    ln_ctr = [0]
    ln_sc = {}
    ln_front_v = {}
    ln_slot = {}

    def ln_front(key, src_fn, slot, xwait=None):
        if xwait is not None:
            sem, val = xwait
            p.wait("vector", sem, val)
            p.wait("scalar", sem, val)
        p.emit("scalar", lambda e, s_=slot: e.activation(out=sqdump[:], in_=src_fn(), func=AF.Square,
                                                         accum_out=sxx[:, s_:s_ + 1]), inc=SC)
        p.wait("vector", SC, slot_mu_sc[slot])
        p.emit("vector", lambda e, s_=slot: e.reduce_sum(sx[:, s_:s_ + 1], src_fn(), axis=AX.X), inc=V)
        p.emit("vector", lambda e, s_=slot: e.tensor_scalar(out=mu[:, s_:s_ + 1], in0=sx[:, s_:s_ + 1],
                                                            scalar1=1.0 / D, scalar2=None, op0=ALU.mult), inc=V)
        p.emit("vector", lambda e, s_=slot: e.tensor_scalar(out=nb2[:, s_:s_ + 1], in0=mu[:, s_:s_ + 1],
                                                            scalar1=mu[:, s_:s_ + 1], scalar2=-1.0,
                                                            op0=ALU.mult, op1=ALU.mult), inc=V)
        ln_front_v[key] = V.n
        ln_slot[key] = slot

    def ln_back(key, src_fn, norm_eng="SC"):
        xslot = ln_ctr[0] % 2
        ln_ctr[0] += 1
        ln_xs[key] = xslot
        slot = ln_slot[key]
        p.wait("scalar", V, ln_front_v[key])
        p.wait("scalar", V, slot_norm_v[slot])
        p.emit("scalar", lambda e, s_=slot: e.activation(out=var[:, s_:s_ + 1], in_=sxx[:, s_:s_ + 1],
                                                         func=AF.Identity, scale=1.0 / D,
                                                         bias=nb2[:, s_:s_ + 1]), inc=SC)
        p.emit("scalar", lambda e, s_=slot: e.activation(out=isg[:, s_:s_ + 1], in_=var[:, s_:s_ + 1],
                                                         func=AF.Ln, bias=epsb[:]), inc=SC)
        p.emit("scalar", lambda e, s_=slot: e.activation(out=isg[:, s_:s_ + 1], in_=isg[:, s_:s_ + 1],
                                                         func=AF.Exp, scale=-0.5), inc=SC)
        p.emit("scalar", lambda e, s_=slot: e.activation(out=var[:, s_:s_ + 1], in_=mu[:, s_:s_ + 1],
                                                         func=AF.Identity, scale=isg[:, s_:s_ + 1]), inc=SC)
        p.emit("scalar", lambda e, s_=slot: e.activation(out=var[:, s_:s_ + 1], in_=var[:, s_:s_ + 1],
                                                         func=AF.Identity, scale=-1.0), inc=SC)
        slot_mu_sc[slot] = SC.n
        if norm_eng == "V":
            p.wait("vector", SC, SC.n)
            p.wait("vector", TRDs[xslot], xslot_tr[xslot])
            p.emit("vector", lambda e, s_=slot, xs=xslot: e.tensor_scalar(
                out=xlnb[xs][:], in0=src_fn(), scalar1=isg[:, s_:s_ + 1],
                scalar2=var[:, s_:s_ + 1], op0=ALU.mult, op1=ALU.add), inc=V)
            slot_norm_v[slot] = V.n
            ln_sc[key] = ("V", V.n)
            return ("V", V.n)
        p.wait("scalar", TRDs[xslot], xslot_tr[xslot])
        p.emit("scalar", lambda e, s_=slot, xs=xslot: e.activation(
            out=xlnb[xs][:], in_=src_fn(), func=AF.Identity,
            scale=isg[:, s_:s_ + 1], bias=var[:, s_:s_ + 1]), inc=SC)
        ln_sc[key] = ("SC", SC.n)
        return ("SC", SC.n)

    pO_free_v = [v_bq32, v_bq32]
    pO_free_sc = [0, 0]

    def emit_transpose(key, rot, dst_fn, sc_ready, t_prior=0):
        xslot = ln_xs[key]
        p.wait("sync", V if sc_ready[0] == "V" else SC, sc_ready[1])
        p.wait("sync", T, t_prior)
        p.emit("sync", lambda e, xs=xslot: e.dma_start_transpose(dst_fn(), xlnb[xs][:]),
               inc=TRDs[xslot], inc_n=16)
        xslot_tr[xslot] = TRDs[xslot].n
        return (TRDs[0].n, TRDs[1].n, TRDs[2].n)

    def xlnT_dst(tensor_h, ncols, tok_off):
        return lambda: tensor_h[:].rearrange("q (dc n) -> q dc n", dc=DC)[:, :, tok_off:tok_off + 128]

    trd_snap = {}
    xload = {}
    xb_free = {}

    def load_x(key, src_ap_fn, xbslot):
        eng_val = xb_free.get(xbslot)
        if eng_val is not None:
            sem, val = eng_val
            p.wait("sync", sem, val)
        p.emit("sync", lambda e: e.dma_start(out=xb[xbslot][:], in_=src_ap_fn()),
               inc=LDX[xbslot], inc_n=16)
        xload[key] = (LDX[xbslot], LDX[xbslot].n, xbslot)

    u = 0
    v_lnT_ev = (0, 0, 0)

    def emit_qproj(g, trd_wait):
        nonlocal u
        p.wait("tensor", WQ, lw_wq)
        half_snap = trd_snap[("q", g * 8 + 3)]
        full_snap = trd_snap[("q", g * 8 + 7)]
        for qc in range(8):
            p.wait("tensor", SC, pO_free_sc[u % 2])
            p.wait("tensor", V, pO_free_v[u % 2])
            for nbq in range(2):
                snap = half_snap if nbq == 0 else full_snap
                for _s in range(3):
                    p.wait("tensor", TRDs[_s], snap[_s])
                for dc in range(DC):
                    p.emit("tensor", lambda e, qc=qc, dc=dc, nbq=nbq, u=u: e.matmul(
                        pO[u % 2][:, nbq * 512:nbq * 512 + 512],
                        wq_sb[:, dc * 1024 + qc * 128: dc * 1024 + qc * 128 + 128],
                        xlnTq[:, dc * 1024 + nbq * 512: dc * 1024 + nbq * 512 + 512],
                        start=(dc == 0), stop=(dc == DC - 1)),
                        inc=(T if (dc == DC - 1 and nbq == 1) else None))
            t_q = T.n
            p.wait("vector", T, t_q)
            p.emit("vector", lambda e, qc=qc, g=g, u=u: e.tensor_scalar(
                out=Q_sb[:, qc * 2048 + g * 1024: qc * 2048 + g * 1024 + 1024],
                in0=pO[u % 2][:], scalar1=1.0 / 32.0, scalar2=bq32[:, qc:qc + 1],
                op0=ALU.mult, op1=ALU.add), inc=V)
            pO_free_v[u % 2] = V.n
            u += 1
        return T.n

    chains = [("q", t) for t in range(NT)] + [("kv", 0, tt) for tt in range(KBN)]
    t_qproj0 = 0

    def chain_src(key):
        if key[0] == "q":
            t = key[1]
            return lambda: xq[t * 128:(t + 1) * 128, :]
        _, kb_, tt = key
        return lambda: xkv[kb_ * 1024 + tt * 128: kb_ * 1024 + tt * 128 + 128, :]

    def chain_dst(key):
        if key[0] == "q":
            return xlnT_dst(xlnTq, 1024, (key[1] % 8) * 128)
        return xlnT_dst(xlnT, 1024, key[2] * 128)

    for i in range(len(chains) + 2):
        if i < len(chains):
            key = chains[i]
            load_x(key, chain_src(key), i % 3)
            sem, val, xbs = xload[key]
            ln_front(key, lambda xbs=xbs: xb[xbs][:], i % NSLOT, xwait=(sem, val))
        if 0 <= i - 1 < len(chains):
            key = chains[i - 1]
            _, _, xbs = xload[key]
            ev_ln = ln_back(key, lambda xbs=xbs: xb[xbs][:], norm_eng="V")
            xb_free[xbs] = (V if ev_ln[0] == "V" else SC, ev_ln[1])
        if 0 <= i - 2 < len(chains):
            key = chains[i - 2]
            v_lnT_ev = trd_snap[key] = emit_transpose(key, (i - 2) % 2, chain_dst(key), ln_sc[key],
                                      t_prior=(t_qproj0 if key[0] == "q" and key[1] >= 8 else 0))
            if key == ("q", 7):
                t_qproj0 = emit_qproj(0, v_lnT_ev)
            elif key == ("q", 15):
                emit_qproj(1, v_lnT_ev)
    t_phaseq = T.n
    sq.close()

    wkv_sb = sattn.enter_context(nc.sbuf_tensor("wkv_sb", [128, DC * 2048], BF16))
    p.wait("gpsimd", T, t_phaseq)
    p.emit("gpsimd", lambda e: e.dma_start(
        out=wkv_sb[:].rearrange("p (dc n) -> p dc n", dc=DC)[:, :, 0:1024],
        in_=wkvh.rearrange("p (dc n) -> p dc n", dc=DC)[:, :, 0:1024],
        max_dma_last_dim=8192), inc=WKV, inc_n=16)
    lw_wk = WKV.n
    p.emit("gpsimd", lambda e: e.dma_start(
        out=wkv_sb[:].rearrange("p (dc n) -> p dc n", dc=DC)[:, :, 1024:2048],
        in_=wkvh.rearrange("p (dc n) -> p dc n", dc=DC)[:, :, 1024:2048],
        max_dma_last_dim=8192), inc=WV, inc_n=16)
    p.emit("gpsimd", lambda e: e.dma_start(out=bvbf[:], in_=bvh), inc=WV, inc_n=16)
    lw_wv = WV.n
    p.wait("sync", T, t_phaseq)
    p.emit("sync", lambda e: e.dma_start(out=masksb[:], in_=maskh), inc=LDM, inc_n=16)
    ld_mask = LDM.n

    spS_free = [("V", 0)] * 2
    kvst = {"v": 0}
    lb_gp = [0, 0]
    nm_gp = [0] * NT
    l_gp = [0] * NT
    vO_tile = [0] * NT
    exp_sc_tile = {}
    upd = 0

    def emit_QK(t, halves, r):
        j = r % 2
        eng, val = spS_free[j]
        p.wait("tensor", V if eng == "V" else SC, val)
        for h in halves:
            for qc in range(8):
                p.emit("tensor", lambda e, t=t, h=h, qc=qc, j=j: e.matmul(
                    sp[j][:, h * 512:h * 512 + 512],
                    Q_sb[:, qc * 2048 + t * 128: qc * 2048 + t * 128 + 128],
                    K_sb[:, qc * 1024 + h * 512: qc * 1024 + h * 512 + 512],
                    start=(qc == 0), stop=(qc == 7)),
                    inc=(T if qc == 7 else None))
        return T.n

    def emit_softmax(t, halves, masked, r, t_qk, sc_prev_parity):
        j = r % 2
        rr = r % 2
        ncol = 512 * len(halves)
        p.wait("vector", T, t_qk)
        p.wait("vector", SC, sc_prev_parity)
        p.wait("vector", GP, lb_gp[rr])
        p.wait("vector", GP, nm_gp[t])
        for h in halves:
            if masked[h]:
                p.emit("vector", lambda e, j=j, h=h: e.tensor_tensor(
                    out=sp[j][:, h * 512:h * 512 + 512], in0=sp[j][:, h * 512:h * 512 + 512],
                    in1=masksb[:], op=ALU.add), inc=V)
        p.emit("vector", lambda e, j=j, rr=rr, ncol=ncol: e.reduce_max(
            lb[:, rr:rr + 1], sp[j][:, 0:ncol], axis=AX.X, negate=True), inc=V)
        p.emit("vector", lambda e, t=t, rr=rr: e.tensor_tensor(
            out=nmnew[:, rr:rr + 1], in0=nm[:, t:t + 1], in1=lb[:, rr:rr + 1], op=ALU.min), inc=V)
        return V.n

    ptr_last = [0, 0]
    av_t = [0, 0]

    def emit_exp(t, halves, r, v_nmnew):
        rr = r % 2
        j = r % 2
        ncol = 512 * len(halves)
        p.wait("scalar", V, v_nmnew)
        p.wait("scalar", PTR[0], ptr_last[0])
        p.wait("scalar", PTR[1], ptr_last[1])
        p.wait("scalar", GP, lb_gp[rr])
        p.wait("scalar", GP, nm_gp[t])
        p.emit("scalar", lambda e, t=t, rr=rr: e.activation(
            out=corr[:, rr:rr + 1], in_=nm[:, t:t + 1], func=AF.Exp,
            scale=-1.0, bias=nmnew[:, rr:rr + 1]), inc=SC)
        p.emit("scalar", lambda e, rr=rr, j=j, ncol=ncol: e.activation(
            out=P_sb[:, 0:ncol], in_=sp[j][:, 0:ncol], func=AF.Exp,
            bias=nmnew[:, rr:rr + 1],
            accum_out=lb[:, rr:rr + 1]), inc=SC)
        spS_free[j] = ("SC", SC.n)
        return SC.n

    def emit_PdmaT(t, halves, r, sc_exp):
        rr = r % 2
        nch = 4 * len(halves)
        p.wait("sync", SC, sc_exp)
        p.wait("sync", T, av_t[rr])
        p.emit("sync", lambda e, rr=rr, nch=nch: e.dma_start_transpose(
            PTd[rr][:].rearrange("q (i n) -> q i n", i=8)[:, 0:nch, 0:128],
            P_sb[:, 0:nch * 128]), inc=PTR[rr], inc_n=16)
        ptr_last[rr] = PTR[rr].n
        return PTR[rr].n

    def emit_AV(t, halves, r, v_pte):
        rr = r % 2
        p.wait("tensor", PTR[rr], v_pte)
        p.wait("tensor", V, kvst["v"])
        p.wait("tensor", V, pO_free_v[t % 2])
        p.wait("tensor", SC, pO_free_sc[t % 2])
        nchunk = 4 * len(halves)
        for i in range(nchunk):
            for nb2_ in range(2):
                p.emit("tensor", lambda e, i=i, nb2_=nb2_, t=t, rr=rr, f=(i == 0), l=(i == nchunk - 1): e.matmul(
                    pO[t % 2][:, nb2_ * 512:nb2_ * 512 + 512],
                    PTd[rr][:, i * 128:i * 128 + 128],
                    V_sb[:, i * 1024 + nb2_ * 512: i * 1024 + nb2_ * 512 + 512],
                    start=f, stop=l),
                    inc=(T if (i == nchunk - 1 and nb2_ == 1) else None))
        av_t[rr] = T.n
        return T.n

    O_seen = set()

    def emit_Ofin(t, halves, r, t_av, sc_exp):
        rr = r % 2
        p.wait("vector", T, t_av)
        p.wait("vector", SC, sc_exp)
        if t not in O_seen:
            O_seen.add(t)
            p.emit("vector", lambda e, t=t: e.tensor_copy(O[t][:], pO[t % 2][:]), inc=V)
        else:
            p.emit("vector", lambda e, t=t, rr=rr: e.scalar_tensor_tensor(
                out=O[t][:], in0=O[t][:],
                scalar=corr[:, rr:rr + 1], in1=pO[t % 2][:],
                op0=ALU.mult, op1=ALU.add), inc=V)
        vO_tile[t] = V.n
        pO_free_v[t % 2] = V.n
        p.wait("gpsimd", SC, sc_exp)
        p.emit("gpsimd", lambda e, t=t, rr=rr: e.tensor_scalar(
            out=lden[:, t:t + 1], in0=lden[:, t:t + 1], scalar1=corr[:, rr:rr + 1],
            scalar2=lb[:, rr:rr + 1], op0=ALU.mult, op1=ALU.add), inc=GP)
        l_gp[t] = GP.n
        p.emit("gpsimd", lambda e, t=t, rr=rr: e.tensor_copy(
            nm[:, t:t + 1], nmnew[:, rr:rr + 1]), inc=GP)
        nm_gp[t] = GP.n
        lb_gp[rr] = GP.n
        return V.n

    v_p3 = [0] * NT
    v_xfT = [(0, 0, 0)] * NT
    t_retire_T = {}

    def run_item_post(it):
        kind = it[0]
        if kind == "p3":
            _, rr_ = it
            ev = xb_free.get(2)
            if ev is not None:
                p.wait("sync", ev[0], ev[1])
            p.emit("sync", lambda e, r=rr_: e.dma_start(out=xb[2][:], in_=xq[r * 128:(r + 1) * 128, :]),
                   inc=LDX[2], inc_n=16)
            ldv = LDX[2].n
            p.wait("vector", GP, l_gp[rr_])
            p.emit("vector", lambda e, r=rr_: e.reciprocal(invl[:, r % 2:r % 2 + 1], lden[:, r:r + 1]), inc=V)
            p.emit("vector", lambda e, r=rr_: e.tensor_scalar(
                out=O[r][:], in0=O[r][:], scalar1=invl[:, r % 2:r % 2 + 1],
                scalar2=None, op0=ALU.mult), inc=V)
            p.wait("vector", LDX[2], ldv)
            p.emit("vector", lambda e, r=rr_: e.tensor_tensor(
                out=O[r][:], in0=O[r][:], in1=xb[2][:], op=ALU.add), inc=V)
            v_p3[rr_] = V.n
            vO_tile[rr_] = V.n
            xb_free[2] = (V, V.n)
        elif kind == "l2f":
            _, rr_ = it
            ln_front(("ln2", rr_), lambda r=rr_: O[r][:], rr_ % NSLOT, xwait=(V, v_p3[rr_]))
        elif kind == "l2b":
            _, rr_ = it
            ln_back(("ln2", rr_), lambda r=rr_: O[r][:])
        elif kind == "l2t":
            _, rr_ = it
            v_xfT[rr_] = emit_transpose(("ln2", rr_), rr_ % 2,
                                        xlnT_dst(Q_sb, 2048, rr_ * 128), ln_sc[("ln2", rr_)],
                                        t_prior=t_retire_T.get(rr_, 0))

    kprojst = {"sc": 0}

    def emit_kproj(kb2, pop=None):
        p.wait("tensor", WKV, lw_wk)
        half_snap = trd_snap.get(("kv", kb2, 3), v_lnT_ev)
        full_snap = trd_snap.get(("kv", kb2, 7), v_lnT_ev)
        rot = 0
        for kc in range(8):
            p.wait("tensor", SC, pO_free_sc[rot])
            p.wait("tensor", V, pO_free_v[rot])
            for nbq in range(2):
                snap = half_snap if nbq == 0 else full_snap
                for _s in range(3):
                    p.wait("tensor", TRDs[_s], snap[_s])
                for dc in range(DC):
                    p.emit("tensor", lambda e, kc=kc, dc=dc, nbq=nbq, rot=rot: e.matmul(
                        pO[rot][:, nbq * 512:nbq * 512 + 512],
                        wkv_sb[:, dc * 2048 + kc * 128: dc * 2048 + kc * 128 + 128],
                        xlnT[:, dc * 1024 + nbq * 512: dc * 1024 + nbq * 512 + 512],
                        start=(dc == 0), stop=(dc == DC - 1)),
                        inc=(T if (dc == DC - 1 and nbq == 1) else None))
            p.wait("scalar", T, T.n)
            p.emit("scalar", lambda e, kc=kc, rot=rot: e.activation(
                out=K_sb[:, kc * 1024:kc * 1024 + 1024], in_=pO[rot][:],
                func=AF.Identity, bias=bksb[:, kc:kc + 1]), inc=SC)
            pO_free_sc[rot] = SC.n
            rot ^= 1
            if pop is not None:
                pop()
        kprojst["sc"] = SC.n
        kprojst["rot"] = rot

    emit_kproj(0)

    for kb in range(KBN):
        rot = kprojst["rot"]

        def emit_vproj():
            nonlocal rot
            p.wait("tensor", WV, lw_wv)
            p.wait("tensor", GP, gp_init)
            for tc in range(8):
                p.wait("tensor", SC, pO_free_sc[rot])
                p.wait("tensor", V, pO_free_v[rot])
                for dc in range(DC):
                    for nbq in range(2):
                        p.emit("tensor", lambda e, tc=tc, dc=dc, nbq=nbq, rot=rot: e.matmul(
                            pO[rot][:, nbq * 512:nbq * 512 + 512],
                            xlnT[:, dc * 1024 + tc * 128: dc * 1024 + tc * 128 + 128],
                            wkv_sb[:, dc * 2048 + 1024 + nbq * 512: dc * 2048 + 1024 + nbq * 512 + 512],
                            start=(dc == 0), stop=False), inc=None)
                for nbq in range(2):
                    p.emit("tensor", lambda e, nbq=nbq, rot=rot: e.matmul(
                        pO[rot][:, nbq * 512:nbq * 512 + 512],
                        ones1[:, 0:128],
                        bvbf[:, nbq * 512:nbq * 512 + 512],
                        start=False, stop=True),
                        inc=(T if nbq == 1 else None))
                p.wait("vector", T, T.n)
                p.emit("vector", lambda e, tc=tc, rot=rot: e.tensor_copy(
                    V_sb[:, tc * 1024:tc * 1024 + 1024], pO[rot][:]), inc=V)
                pO_free_v[rot] = V.n
                rot ^= 1
            kvst["v"] = V.n
            kvst["t_read"] = T.n
        kvst["v"] = 0
        p.wait("tensor", SC, kprojst["sc"])

        tiles = list(range(2 * kb, NT))
        n_it = len(tiles)
        items = []
        if kb < KBN - 1:
            nkb = kb + 1
            for j in range(KBN + 2):
                if j < KBN:
                    items.append(("kvf", nkb, j))
                if 0 <= j - 1 < KBN:
                    items.append(("kvb", nkb, j - 1))
                if 0 <= j - 2 < KBN:
                    items.append(("kvt", nkb, j - 2))
        if kb >= 1:
            r0, r1 = 2 * kb - 2, 2 * kb - 1
            t_retire_T[r0] = T.n
            t_retire_T[r1] = T.n
            items += [("p3", r0), ("l2f", r0), ("l2b", r0), ("p3", r1),
                      ("l2t", r0), ("l2f", r1), ("l2b", r1), ("l2t", r1)]

        def run_item(it):
            nonlocal v_lnT_ev
            kind = it[0]
            if kind == "kvf":
                _, nkb, tt = it
                load_x(("kv", nkb, tt), lambda nkb=nkb, tt=tt: xkv[nkb * 1024 + tt * 128: nkb * 1024 + tt * 128 + 128, :], tt % 3)
                sem, val, xbs = xload[("kv", nkb, tt)]
                ln_front(("kv", nkb, tt), lambda xbs=xbs: xb[xbs][:], tt % NSLOT, xwait=(sem, val))
            elif kind == "kvb":
                _, nkb, tt = it
                _, _, xbs = xload[("kv", nkb, tt)]
                ev_ln = ln_back(("kv", nkb, tt), lambda xbs=xbs: xb[xbs][:])
                xb_free[xbs] = (V if ev_ln[0] == "V" else SC, ev_ln[1])
            elif kind == "kvt":
                _, nkb, tt = it
                v_lnT_ev = trd_snap[("kv", nkb, tt)] = emit_transpose(("kv", nkb, tt), tt % 2,
                                          xlnT_dst(xlnT, 1024, tt * 128), ln_sc[("kv", nkb, tt)],
                                          t_prior=kvst.get("t_read", 0))
            else:
                run_item_post(it)

        pend = []
        sc_exp_parity = [0, 0]
        n_kv_items = sum(1 for it in items if it[0].startswith("kv"))
        qidx = 0
        while qidx < min(4, len(items)):
            run_item(items[qidx]); qidx += 1
        for i, t in enumerate(tiles):
            halves = [0] if i == 0 else [0, 1]
            masked = {0: (i == 0), 1: (i == 1)}
            r = upd
            if kb == 0 and i == 0:
                p.wait("vector", LDM, ld_mask)
            t_qk = emit_QK(t, halves, r)
            if len(pend) >= 2:
                at, ah, ar, asc, apte = pend.pop(0)
                t_av = emit_AV(at, ah, ar, apte)
                emit_Ofin(at, ah, ar, t_av, asc)
            if pend:
                pt_, ph, pr, psc, _ = pend[-1]
                v_pte = emit_PdmaT(pt_, ph, pr, psc)
                pend[-1] = (pt_, ph, pr, psc, v_pte)
            v_nmnew = emit_softmax(t, halves, masked, r, t_qk, sc_exp_parity[r % 2])
            sc_exp = emit_exp(t, halves, r, v_nmnew)
            sc_exp_parity[r % 2] = sc_exp
            pend.append((t, halves, r, sc_exp, None))
            upd += 1
            if i == 0:
                emit_vproj()
            iters_left = n_it - 1 - i
            kv_left = max(0, n_kv_items - qidx)
            npop = max(3, -(-kv_left // iters_left)) if iters_left > 0 else 3
            for _ in range(npop):
                if qidx < len(items):
                    run_item(items[qidx]); qidx += 1
        if len(pend) >= 2:
            at, ah, ar, asc, apte = pend.pop(0)
            t_av = emit_AV(at, ah, ar, apte)
            emit_Ofin(at, ah, ar, t_av, asc)
        pt_, ph, pr, psc, _ = pend[-1]
        v_pte = emit_PdmaT(pt_, ph, pr, psc)
        while qidx < n_kv_items:
            run_item(items[qidx]); qidx += 1
        t_av = emit_AV(pt_, ph, pr, v_pte)
        emit_Ofin(pt_, ph, pr, t_av, psc)
        pend = []
        if kb < KBN - 1:
            def pop_hook():
                nonlocal qidx
                if qidx < len(items):
                    run_item(items[qidx]); qidx += 1
            emit_kproj(kb + 1, pop=pop_hook)
        while qidx < len(items):
            run_item(items[qidx]); qidx += 1
    for rr_ in (14, 15):
        t_retire_T[rr_] = T.n
        for kind in ("p3", "l2f", "l2b", "l2t"):
            run_item_post((kind, rr_))
    trd_all_xfT = (TRDs[0].n, TRDs[1].n, TRDs[2].n)
    t_attn = T.n
    sc_attn = SC.n

    if stage == "attn":
        for t in range(NT):
            p.wait("sync", V, vO_tile[t])
            p.emit("sync", lambda e, t=t: e.dma_start(out=out[t * 128:(t + 1) * 128, :], in_=O[t][:]),
                   inc=ST, inc_n=16)
        p.wait("sync", ST, ST.n)
        replay(nc, p)
        sattn.close(); skeep.close(); es.close(); ses.close()
        return nc

    xfT = Q_sb
    sc_rexp = [0, 0]
    v_router_t = [0] * NT

    r4_sc = [0, 0, 0, 0]
    r4_v = [0, 0, 0, 0]

    def emit_router(t, v_ev, nb4=False):
        j = t % 4 if nb4 else t % 2
        bank = [pO[0], pO[1], sp[0], sp[1]][j] if nb4 else pO[j]
        for _s in range(3):
            p.wait("tensor", TRDs[_s], v_ev[_s])
        p.wait("tensor", RG, lw_rw)
        if nb4:
            p.wait("tensor", SC, r4_sc[j])
            p.wait("tensor", V, r4_v[j])
        else:
            p.wait("tensor", V, pO_free_v[j])
            p.wait("tensor", SC, pO_free_sc[j])
            p.wait("tensor", SC, sc_rexp[j])
        for dc in range(DC):
            p.emit("tensor", lambda e, t=t, dc=dc, bank=bank: e.matmul(
                bank[:, 0:9], xfT[:, dc * 2048 + t * 128: dc * 2048 + t * 128 + 128],
                rwg_sb[:, dc * 9:dc * 9 + 9],
                start=(dc == 0), stop=(dc == DC - 1)), inc=(T if dc == DC - 1 else None))
        t_rt = T.n
        pe = rsc[:, j * 24: j * 24 + 8]
        pe2 = rsc[:, j * 24 + 8: j * 24 + 16]
        sm = rsc[:, j * 24 + 16: j * 24 + 24]
        p.wait("vector", T, t_rt)
        p.wait("vector", SC, r4_sc[j] if nb4 else sc_rexp[j])
        p.emit("vector", lambda e, sm=sm, bank=bank: e.reduce_max(sm[:, 0:1], bank[:, 0:8],
                                                            axis=AX.X, negate=True), inc=V)
        v_rmax = V.n
        if nb4:
            r4_v[j] = V.n
        else:
            pO_free_v[j] = V.n
        p.wait("scalar", V, v_rmax)
        p.emit("scalar", lambda e, pe=pe, sm=sm, bank=bank: e.activation(
            out=pe, in_=bank[:, 0:8], func=AF.Exp, bias=sm[:, 0:1],
            accum_out=sm[:, 1:2]), inc=SC)
        p.emit("scalar", lambda e, t=t, bank=bank: e.activation(
            out=g_sb[:, t:t + 1], in_=bank[:, 8:9], func=AF.Sigmoid), inc=SC)
        if nb4:
            r4_sc[j] = SC.n
        else:
            sc_rexp[j] = SC.n
            pO_free_sc[j] = SC.n
        p.wait("vector", SC, SC.n)
        p.emit("vector", lambda e, pe=pe, sm=sm: e.reduce_max(sm[:, 2:3], pe, axis=AX.X), inc=V)
        p.emit("vector", lambda e, pe=pe, pe2=pe2, sm=sm: e.scalar_tensor_tensor(
            out=pe2, in0=pe, scalar=sm[:, 2:3], in1=pe,
            op0=ALU.is_lt, op1=ALU.mult), inc=V)
        p.emit("vector", lambda e, pe2=pe2, sm=sm: e.reduce_max(sm[:, 3:4], pe2, axis=AX.X), inc=V)
        p.emit("vector", lambda e, pe=pe, pe2=pe2, sm=sm: e.scalar_tensor_tensor(
            out=pe2, in0=pe, scalar=sm[:, 3:4], in1=pe,
            op0=ALU.is_ge, op1=ALU.mult), inc=V)
        p.emit("vector", lambda e, sm=sm: e.reciprocal(sm[:, 4:5], sm[:, 1:2]), inc=V)
        p.emit("vector", lambda e, t=t, pe2=pe2, sm=sm: e.tensor_scalar(
            out=c_sb[:, t * 8:t * 8 + 8], in0=pe2, scalar1=sm[:, 4:5],
            scalar2=None, op0=ALU.mult), inc=V)
        v_router_t[t] = V.n

    for jj in range(2):
        eng, val = spS_free[jj]
        p.wait("tensor", V if eng == "V" else SC, val)
    p.wait("tensor", V, pO_free_v[0]); p.wait("tensor", V, pO_free_v[1])
    p.wait("tensor", SC, pO_free_sc[0]); p.wait("tensor", SC, pO_free_sc[1])
    for t in range(8):
        emit_router(t, v_xfT[t], nb4=True)
    pO_free_sc[0] = r4_sc[0]; pO_free_sc[1] = r4_sc[1]
    pO_free_v[0] = r4_v[0]; pO_free_v[1] = r4_v[1]
    sc_rexp[0] = r4_sc[0]; sc_rexp[1] = r4_sc[1]
    spS_free[0] = ("SC", r4_sc[2]); spS_free[1] = ("SC", r4_sc[3])
    v_router8 = V.n
    sattn.close()

    smoe = ExitStack()
    sbm = lambda name, shape, dty=F32: smoe.enter_context(nc.sbuf_tensor(name, shape, dty))
    wgu_sb = [sbm(f"wgu{i}", [128, 2048], BF16) for i in range(2)]
    dw_sb = sbm("dw_sb", [128, 8192], BF16)
    hb = [sbm(f"hb{i}", [128, 8192], BF16) for i in range(2)]
    w1_sb = [sbm(f"w1_{i}", [128, 8192], BF16) for i in range(2)]
    w2_sb = sbm("w2_sb", [128, 8192], BF16)
    p.wait("gpsimd", T, t_attn)
    p.wait("gpsimd", SC, sc_attn)
    for _s in range(3):
        p.wait("gpsimd", TRDs[_s], trd_all_xfT[_s])
    p.wait("gpsimd", PTR[0], ptr_last[0])
    p.wait("gpsimd", PTR[1], ptr_last[1])

    gu_ld = {}

    def load_wgu(fn):
        li = fn % 2
        p.emit("gpsimd", lambda e, fn=fn, li=li: e.dma_start(
            out=wgu_sb[li][:, 0:1024], in_=wguh[:, fn * 2048:fn * 2048 + 1024],
            max_dma_last_dim=8192), inc=GU[li], inc_n=16)
        p.emit("gpsimd", lambda e, fn=fn, li=li: e.dma_start(
            out=wgu_sb[li][:, 1024:2048], in_=wguh[:, fn * 2048 + 1024:fn * 2048 + 2048],
            max_dma_last_dim=8192), inc=GUB[li], inc_n=16)
        gu_ld[fn] = (GU[li].n, GUB[li].n)

    for f0 in range(2):
        load_wgu(f0)
    p.emit("gpsimd", lambda e: e.dma_start(out=dw_sb[:], in_=dwh, max_dma_last_dim=8192), inc=DW, inc_n=16)
    dw_ld = DW.n
    w1_ld = {}
    w2_ld = {}
    for e0 in range(2):
        p.emit("gpsimd", lambda e, e0=e0: e.dma_start(
            out=w1_sb[e0][:], in_=w1p[e0], max_dma_last_dim=8192), inc=W1S[e0], inc_n=16)
        w1_ld[e0] = W1S[e0].n
    p.emit("gpsimd", lambda e: e.dma_start(
        out=w2_sb[:], in_=w2p[0], max_dma_last_dim=8192), inc=W2S, inc_n=16)
    w2_ld[0] = W2S.n

    t_wgu_free = [0, 0]
    v_hb = [0, 0]
    p.wait("tensor", V, v_router8)
    for fc in range(8):
        li = fc % 2
        for th in range(2):
            for _s in range(3):
                p.wait("tensor", TRDs[_s], trd_all_xfT[_s])
            for jj in range(2):
                eng, val = spS_free[jj]
                p.wait("tensor", V if eng == "V" else SC, val)
            for dh in range(2):
                p.wait("tensor", (GU if dh == 0 else GUB)[li], gu_ld[fc][dh])
                for dc in range(dh * 4, dh * 4 + 4):
                    for nbq in range(2):
                        p.emit("tensor", lambda e, fc=fc, th=th, dc=dc, nbq=nbq, li=li: e.matmul(
                            sp[0][:, nbq * 512:nbq * 512 + 512],
                            wgu_sb[li][:, dc * 256: dc * 256 + 128],
                            xfT[:, dc * 2048 + th * 1024 + nbq * 512: dc * 2048 + th * 1024 + nbq * 512 + 512],
                            start=(dc == 0), stop=(dc == DC - 1)), inc=None)
                        p.emit("tensor", lambda e, fc=fc, th=th, dc=dc, nbq=nbq, li=li: e.matmul(
                            sp[1][:, nbq * 512:nbq * 512 + 512],
                            wgu_sb[li][:, dc * 256 + 128: dc * 256 + 256],
                            xfT[:, dc * 2048 + th * 1024 + nbq * 512: dc * 2048 + th * 1024 + nbq * 512 + 512],
                            start=(dc == 0), stop=(dc == DC - 1)),
                            inc=(T if (dc == DC - 1 and nbq == 1) else None))
            t_wgu_free[li] = T.n
            if th == 0:
                emit_router(8 + fc, v_xfT[8 + fc])
            p.wait("scalar", T, T.n)
            p.emit("scalar", lambda e: e.activation(
                out=sqdump[:], in_=sp[0][:], func=AF.Silu), inc=SC)
            sc_sil = SC.n
            p.wait("vector", SC, sc_sil)
            p.emit("vector", lambda e, fc=fc, th=th: e.tensor_tensor(
                out=hb[th][:, fc * 1024: fc * 1024 + 1024],
                in0=sqdump[:], in1=sp[1][:], op=ALU.mult), inc=V)
            spS_free[0] = ("SC", sc_sil)
            spS_free[1] = ("V", V.n)
            v_hb[th] = V.n
        if fc + 2 < 8:
            fn = fc + 2
            p.wait("gpsimd", T, t_wgu_free[fn % 2])
            load_wgu(fn)

    for th in range(2):
        p.wait("tensor", DW, dw_ld)
        p.wait("tensor", V, v_hb[th])
        for tc in range(8):
            tl = th * 8 + tc
            p.wait("tensor", V, pO_free_v[tc % 2])
            p.wait("tensor", SC, pO_free_sc[tc % 2])
            for fc in range(8):
                for nbq in range(2):
                    p.emit("tensor", lambda e, th=th, tc=tc, fc=fc, nbq=nbq: e.matmul(
                        pO[tc % 2][:, nbq * 512:nbq * 512 + 512],
                        hb[th][:, fc * 1024 + tc * 128: fc * 1024 + tc * 128 + 128],
                        dw_sb[:, fc * 1024 + nbq * 512: fc * 1024 + nbq * 512 + 512],
                        start=(fc == 0), stop=(fc == 7)),
                        inc=(T if (fc == 7 and nbq == 1) else None))
            t_dn = T.n
            p.wait("vector", T, t_dn)
            p.emit("vector", lambda e, tl=tl, tc=tc: e.scalar_tensor_tensor(
                out=O[tl][:], in0=pO[tc % 2][:], scalar=g_sb[:, tl:tl + 1],
                in1=O[tl][:], op0=ALU.mult, op1=ALU.add), inc=V)
            pO_free_v[tc % 2] = V.n
            vO_tile[tl] = V.n
    t_down_prev = T.n

    t_h_free = [t_down_prev, t_down_prev]
    t_ye_last = t_down_prev
    for e_i in range(8):
        par = e_i % 2
        lw_w1 = w1_ld[e_i]
        lw_w2 = w2_ld[e_i]
        sc_h_th = [0, 0]
        for th in range(2):
            p.wait("tensor", W1S[par], lw_w1)
            for fc in range(8):
                jj = fc % 2
                eng, val = spS_free[jj]
                p.wait("tensor", V if eng == "V" else SC, val)
                for dc in range(DC):
                    for nbq in range(2):
                        p.emit("tensor", lambda e, e_i=e_i, th=th, fc=fc, dc=dc, nbq=nbq, jj=jj: e.matmul(
                            sp[jj][:, nbq * 512:nbq * 512 + 512],
                            w1_sb[e_i % 2][:, dc * 1024 + fc * 128: dc * 1024 + fc * 128 + 128],
                            xfT[:, dc * 2048 + th * 1024 + nbq * 512: dc * 2048 + th * 1024 + nbq * 512 + 512],
                            start=(dc == 0), stop=(dc == DC - 1)),
                            inc=(T if (dc == DC - 1 and nbq == 1) else None))
                t_h_free[par] = T.n
                p.wait("scalar", T, T.n)
                p.emit("scalar", lambda e, th=th, fc=fc, jj=jj: e.activation(
                    out=hb[th][:, fc * 1024: fc * 1024 + 1024],
                    in_=sp[jj][:], func=AF.Silu), inc=SC)
                spS_free[jj] = ("SC", SC.n)
            sc_h_th[th] = SC.n
        for th in range(2):
            p.wait("tensor", W2S, lw_w2)
            p.wait("tensor", SC, sc_h_th[th])
            for tc in range(8):
                tl = th * 8 + tc
                p.wait("tensor", V, pO_free_v[tc % 2])
                for fc in range(8):
                    for nbq in range(2):
                        p.emit("tensor", lambda e, th=th, tc=tc, fc=fc, nbq=nbq: e.matmul(
                            pO[tc % 2][:, nbq * 512:nbq * 512 + 512],
                            hb[th][:, fc * 1024 + tc * 128: fc * 1024 + tc * 128 + 128],
                            w2_sb[:, fc * 1024 + nbq * 512: fc * 1024 + nbq * 512 + 512],
                            start=(fc == 0), stop=(fc == 7)),
                            inc=(T if (fc == 7 and nbq == 1) else None))
                t_ye_last = T.n
                p.wait("vector", T, t_ye_last)
                p.wait("vector", V, v_router_t[tl])
                p.emit("vector", lambda e, tl=tl, tc=tc, e_i=e_i: e.scalar_tensor_tensor(
                    out=O[tl][:], in0=pO[tc % 2][:], scalar=c_sb[:, tl * 8 + e_i: tl * 8 + e_i + 1],
                    in1=O[tl][:], op0=ALU.mult, op1=ALU.add), inc=V)
                pO_free_v[tc % 2] = V.n
                vO_tile[tl] = V.n
                if e_i == 7:
                    p.wait("sync", V, vO_tile[tl])
                    p.emit("sync", lambda e, tl=tl: e.dma_start(
                        out=out[tl * 128:(tl + 1) * 128, :], in_=O[tl][:]), inc=ST, inc_n=16)
        if e_i + 2 < 8:
            p.wait("gpsimd", T, t_h_free[par])
            p.emit("gpsimd", lambda e, e_i=e_i, par=par: e.dma_start(
                out=w1_sb[par][:], in_=w1p[e_i + 2], max_dma_last_dim=8192), inc=W1S[par], inc_n=16)
            w1_ld[e_i + 2] = W1S[par].n
        if e_i + 1 < 8:
            p.wait("gpsimd", T, t_ye_last)
            p.emit("gpsimd", lambda e, e_i=e_i: e.dma_start(
                out=w2_sb[:], in_=w2p[e_i + 1], max_dma_last_dim=8192), inc=W2S, inc_n=16)
            w2_ld[e_i + 1] = W2S.n

    p.wait("sync", ST, ST.n)
    replay(nc, p)
    smoe.close(); skeep.close(); es.close(); ses.close()
    return nc


def replay(nc, p):
    with nc.Block() as block:
        @block.sync
        def _(e):
            for fn in p.streams["sync"]:
                fn(e)

        @block.gpsimd
        def _(e):
            for fn in p.streams["gpsimd"]:
                fn(e)

        @block.vector
        def _(e):
            for fn in p.streams["vector"]:
                fn(e)

        @block.scalar
        def _(e):
            for fn in p.streams["scalar"]:
                fn(e)

        @block.tensor
        def _(e):
            for fn in p.streams["tensor"]:
                fn(e)



def _prep_shared(W, b, router_w, w1, w2, sgu, sdown, sgw):
    W = np.asarray(W, np.float32)
    wqh = np.ascontiguousarray(W[:, :1024].reshape(8, 128, 1024).transpose(1, 0, 2).reshape(128, 8192))
    wkvh = np.ascontiguousarray(W[:, 1024:3072].reshape(8, 128, 2048).transpose(1, 0, 2).reshape(128, 16384))
    b = np.asarray(b, np.float32)
    bqh = np.ascontiguousarray(b[0:1024].reshape(8, 128).T)
    bkh = np.ascontiguousarray(b[1024:2048].reshape(8, 128).T)
    bvh = np.ascontiguousarray(b[2048:3072].reshape(1, 1024))
    rg = np.concatenate([np.asarray(router_w, np.float32), np.asarray(sgw, np.float32)], axis=1)
    rwgh = np.ascontiguousarray(rg.reshape(8, 128, 9).transpose(1, 0, 2).reshape(128, 72))
    w1p = np.ascontiguousarray(np.asarray(w1, np.float32).reshape(8, 8, 128, 1024).transpose(0, 2, 1, 3).reshape(8, 128, 8192))
    w2p = np.ascontiguousarray(np.asarray(w2, np.float32).reshape(8, 8, 128, 1024).transpose(0, 2, 1, 3).reshape(8, 128, 8192))
    wguh = np.ascontiguousarray(np.asarray(sgu, np.float32).reshape(8, 128, 2, 8, 128).transpose(1, 3, 0, 2, 4).reshape(128, 16384))
    dwh = np.ascontiguousarray(np.asarray(sdown, np.float32).reshape(8, 128, 1024).transpose(1, 0, 2).reshape(128, 8192))
    return dict(wqh=wqh, wkvh=wkvh, bqh=bqh, bkh=bkh, bvh=bvh, rwgh=rwgh,
                w1p=w1p, w2p=w2p, wguh=wguh, dwh=dwh)


def _mask_for(c):
    m = np.zeros((128, 512), np.float32)
    i = np.arange(128)[:, None]
    jj = np.arange(128)[None, :]
    for s in range(4):
        if s == c:
            m[:, s * 128:(s + 1) * 128] = np.where(jj <= i, 0.0, NEG_BIG)
        elif s > c:
            m[:, s * 128:(s + 1) * 128] = NEG_BIG
    return m


def kernel(L0_attn_x, L0_attn_W, L0_attn_b, L0_ffn_router_w, L0_ffn_w1, L0_ffn_w2,
           L0_ffn_shared_gate_up, L0_ffn_shared_down, L0_ffn_shared_gate_w):
    import ml_dtypes
    x = np.asarray(L0_attn_x, np.float32)
    shared = _prep_shared(L0_attn_W, L0_attn_b, L0_ffn_router_w, L0_ffn_w1, L0_ffn_w2,
                          L0_ffn_shared_gate_up, L0_ffn_shared_down, L0_ffn_shared_gate_w)
    ident = np.eye(128, dtype=ml_dtypes.bfloat16)

    nc = build_program(_BUILD_STAGE)

    in_maps = []
    for core in range(8):
        b, c = core // 4, core % 4
        xb_full = np.ascontiguousarray(x[b])
        xqc = np.ascontiguousarray(x[b].reshape(64, 128, 1024)[c::4].reshape(2048, 1024))
        in_maps.append(dict(xkv=xb_full, xq=xqc, maskh=_mask_for(c).astype(ml_dtypes.bfloat16), identh=ident, **shared))

    res = run_bass_kernel_spmd(nc, in_maps, core_ids=list(range(8)))

    outf = np.empty((2, 8192, 1024), np.float32)
    for core in range(8):
        b, c = core // 4, core % 4
        o = np.asarray(res.results[core]["out"]).reshape(16, 128, 1024)
        outf.reshape(2, 64, 128, 1024)[b, c::4] = o
    return outf
